# version 01
# baseline (speedup 1.0000x reference)
import numpy as np
from contextlib import ExitStack

import concourse.bass as bass
import concourse.mybir as mybir
from concourse.bass_utils import run_bass_kernel_spmd

F32 = mybir.dt.float32
AF = mybir.ActivationFunctionType
ALU = mybir.AluOpType

EPS_LN = 1e-5
EPS_RMS = 1e-5
EPS_Q = 1e-8


class CFG:
    def __init__(self, DM=2048, DIN=2048, DXB=512, DTR=128, DFF=5632,
                 TOWN=1024, TC=256):
        self.DM, self.DIN, self.DXB, self.DTR, self.DFF = DM, DIN, DXB, DTR, DFF
        self.TOWN, self.TC = TOWN, TC
        self.G = DIN // 32
        self.GKV = DXB // 32
        assert self.G == 4 * self.GKV
        self.KT = DM // 128
        self.PROJ = 2 * DIN + 2 * DXB + DTR
        self.OT_IN = self.PROJ // 128
        self.OT_Z = DIN // 128
        self.OT_X = DXB // 128
        self.OT_FF = DFF // 128
        self.KT_FF = DFF // 128
        self.DIN_T = DIN // 128
        self.NQ = self.G // 4
        self.NCH = TOWN // TC
        self.o_x0 = self.OT_Z
        self.o_b0 = self.OT_Z + self.OT_X
        self.o_c0 = self.OT_Z + 2 * self.OT_X
        self.o_dt = self.OT_IN - 1
        assert DTR == 128 and self.o_dt == self.o_c0 + self.DIN_T



def tile_w(wT, K, O):
    kt, ot = K // 128, O // 128
    return np.ascontiguousarray(
        np.asarray(wT, np.float32).reshape(kt, 128, ot, 128).transpose(2, 0, 1, 3))


def host_constants(C):
    d = {}
    ep = np.zeros((8, 32, 128), np.float32)
    for j in range(8):
        for r in range(128):
            ep[j, 4 * j + r // 32, r] = 1.0
    ep2 = np.ascontiguousarray(ep.transpose(1, 0, 2).reshape(32, 8 * 128))
    d["c_ep"] = np.ascontiguousarray(np.tile(ep2, (4, 1)))
    srp = np.zeros((8, 128, 224), np.float32)
    for j in range(8):
        for k in range(128):
            srp[j, k, 96 + 4 * j + k // 32] = 1.0
    d["c_sr"] = np.ascontiguousarray(srp.transpose(1, 0, 2).reshape(128, 8 * 224))
    d["c_ones_k"] = np.ones((128, 1), np.float32)
    d["c_ones_1"] = np.ones((1, 128), np.float32)
    misc = np.zeros((128, 8), np.float32)
    misc[:, 0] = 1.0 / C.DM
    misc[:, 1] = EPS_LN
    misc[:, 2] = 0.5 / (C.DM * C.DFF)
    misc[:, 3] = 0.5 * EPS_Q
    misc[:, 4] = 0.5 / (C.KT_FF * 128 * C.DM)
    misc[:, 5] = -1.0
    d["c_misc"] = misc
    return d


def host_weights(C, inputs):
    d = {}
    d["w_in"] = tile_w(np.asarray(inputs["L0_attn_in_proj_w"]).T, C.DM, C.PROJ)
    d["w_dt"] = tile_w(np.asarray(inputs["L0_attn_dt_proj_w"]).T, C.DTR, C.DIN)
    d["w_o"] = tile_w(np.asarray(inputs["L0_attn_out_proj_w"]).T, C.DIN, C.DM)
    d["w_g"] = tile_w(np.asarray(inputs["L0_attn_gate_w"]).T, C.DM, C.DFF)
    d["w_u"] = tile_w(np.asarray(inputs["L0_attn_up_w"]).T, C.DM, C.DFF)
    d["w_d"] = tile_w(np.asarray(inputs["L0_attn_down_w"]).T, C.DFF, C.DM)
    d["w_f1"] = np.ascontiguousarray(
        np.asarray(inputs["L0_ffn_w1"], np.float32).T.reshape(C.KT, 128, C.DFF))
    d["w_f2"] = np.ascontiguousarray(
        np.asarray(inputs["L0_ffn_w2"], np.float32).T.reshape(C.KT, 128, C.DFF))
    d["w_f3"] = np.ascontiguousarray(
        np.asarray(inputs["L0_ffn_w3"], np.float32).T.reshape(C.KT_FF, 128, C.DM))
    alog = np.asarray(inputs["L0_attn_A_log"], np.float32)
    ac = np.zeros((128, C.G * 8), np.float32)
    for g in range(C.G):
        for j in range(8):
            for r in range(128):
                ac[r, g * 8 + j] = alog[32 * g + 4 * j + r // 32, r % 32]
    d["a_cols"] = ac

    def colT(v, n):
        return np.ascontiguousarray(np.asarray(v, np.float32).reshape(n // 128, 128).T)
    d["c_dtb"] = colT(inputs["L0_attn_dt_proj_b"], C.DIN)
    d["c_D"] = colT(inputs["L0_attn_D_param"], C.DIN)
    d["c_n1w"] = colT(inputs["L0_attn_norm1_w"], C.DM)
    d["c_n2w"] = colT(inputs["L0_attn_norm2_w"], C.DM)
    return d


def host_core_inputs(C, inputs, core, SEQ):
    b, half = divmod(core, 2)
    x = np.asarray(inputs["L0_attn_hidden_states"], np.float32)
    xT = np.ascontiguousarray(x[b].T)
    t0 = C.TOWN * half
    return {
        "x_own": np.ascontiguousarray(xT[:, t0:t0 + C.TOWN]),
        "x_pre": (np.ascontiguousarray(xT[:, 0:C.TOWN]) if half == 1
                  else np.zeros((C.DM, C.TOWN), np.float32)),
    }



ENGINES = ("sync", "tensor", "vector", "scalar", "gpsimd")
ESEMS = ("pe", "dv", "sc", "gp", "qs", "qg", "qa")


class Cell:
    __slots__ = ("v",)

    def __init__(self):
        self.v = None


QENG = {"qs": "sync", "qg": "gpsimd", "qa": "scalar"}
FENCE_EVERY = {"qs": 2, "qg": 8, "qa": 2}


def _rv(x):
    return x.v if isinstance(x, Cell) else x


class Prog:
    def __init__(self):
        self.sched = {e: [] for e in ENGINES}
        self.cnt = {s: 0 for s in ESEMS}
        self.pending = {q: [] for q in QENG}

    def tok(self, sem):
        return (sem, self.cnt[sem])

    def emit(self, eng, fn):
        self.sched[eng].append(fn)

    OWN = {"vector": "dv", "scalar": "sc", "gpsimd": "gp"}

    def op(self, eng, make, incs=(), waits=()):
        waits = [w for w in waits if w is not None]
        own = self.OWN.get(eng)
        if own is not None and self.cnt[own] > 0 and any(s == own for s, _ in incs):
            waits.append((own, self.cnt[own]))
        targets = []
        for (s, n) in incs:
            self.cnt[s] += n
            targets.append((s, self.cnt[s]))

        def fn(e, sems):
            for (s, v) in waits:
                v = _rv(v)
                if v:
                    e.wait_ge(sems[s], v)
            ins = make(e)
            for (s, v) in targets:
                ins.then_inc(sems[s], 16 if s.startswith("q") else 1)
        self.emit(eng, fn)
        return targets[0] if targets else None

    def wait(self, eng, *toks):
        toks = [t for t in toks if t is not None]
        if not toks:
            return

        def fn(e, sems):
            for (s, v) in toks:
                v = _rv(v)
                if v:
                    e.wait_ge(sems[s], v)
        self.emit(eng, fn)

    def fence(self, queue):
        if not self.pending[queue]:
            return
        total = self.cnt[queue]
        for c in self.pending[queue]:
            c.v = total
        self.pending[queue].clear()
        self.wait(QENG[queue], (queue, total))

    def dma(self, queue, out_ap, in_ap, waits=()):
        if any(w is not None for w in waits):
            self.fence(queue)
        tok = self.op(QENG[queue], lambda e: e.dma_start(out=out_ap(), in_=in_ap()),
                      incs=[(queue, 16)], waits=waits)
        cell = Cell()
        self.pending[queue].append(cell)
        if len(self.pending[queue]) >= FENCE_EVERY[queue]:
            self.fence(queue)
        return (queue, cell)

    def qtok(self, queue):
        self.fence(queue)
        return (queue, self.cnt[queue])

    def barrier(self):
        for q in QENG:
            self.fence(q)
        toks = [self.tok(s) for s in ESEMS]
        for e in ENGINES:
            self.wait(e, *toks)


def build(C, debug_outputs=(), n_pre_chunks=None):
    nc = bass.Bass("TRN2", target_bir_lowering=False, debug=False)
    P = Prog()
    TC, KT, DIN_T, NQ = C.TC, C.KT, C.DIN_T, C.NQ
    io = {}

    def din(name, shape):
        io[name] = nc.dram_tensor(name, list(shape), F32, kind="ExternalInput")

    def dout(name, shape):
        io[name] = nc.dram_tensor(name, list(shape), F32, kind="ExternalOutput")

    din("x_own", (C.DM, C.TOWN)); din("x_pre", (C.DM, C.TOWN))
    din("w_in", (C.OT_IN, KT, 128, 128)); din("w_dt", (DIN_T, 1, 128, 128))
    din("w_o", (KT, DIN_T, 128, 128))
    din("w_g", (C.OT_FF, KT, 128, 128)); din("w_u", (C.OT_FF, KT, 128, 128))
    din("w_d", (KT, C.KT_FF, 128, 128))
    din("w_f1", (KT, 128, C.DFF)); din("w_f2", (KT, 128, C.DFF))
    din("w_f3", (C.KT_FF, 128, C.DM))
    din("a_cols", (128, C.G * 8))
    din("c_dtb", (128, DIN_T)); din("c_D", (128, DIN_T))
    din("c_n1w", (128, KT)); din("c_n2w", (128, KT))
    din("c_ep", (128, 8 * 128)); din("c_sr", (128, 8 * 224))
    din("c_ones_k", (128, 1)); din("c_ones_1", (1, 128))
    din("c_misc", (128, 8))
    dout("y_out", (C.DM, C.TOWN))
    qscr = {
        "q1": nc.dram_tensor("q1s", [C.OT_FF, KT, 128, 128], F32, kind="Internal"),
        "q2": nc.dram_tensor("q2s", [C.OT_FF, KT, 128, 128], F32, kind="Internal"),
        "q3": nc.dram_tensor("q3s", [KT, C.KT_FF, 128, 128], F32, kind="Internal"),
    }

    ctx = ExitStack()
    sb, ps = {}, {}

    def sbuf(name, shape, dtype=F32):
        sb[name] = ctx.enter_context(nc.sbuf_tensor(name, list(shape), dtype))

    sbuf("xb", (128, KT * TC))
    sbuf("sq", (128, 2 * TC))
    sbuf("ln1", (128, KT * TC))
    sbuf("u", (128, KT * TC))
    sbuf("zs", (128, 2 * TC))
    sbuf("xv", (128, C.OT_X * TC))
    sbuf("bv", (128, C.OT_X * TC))
    sbuf("cv", (128, 2 * TC))
    sbuf("dtraw", (128, TC))
    sbuf("dt", (128, 2 * TC))
    sbuf("dtx", (128, 2 * TC))
    sbuf("xrep", (128, 2 * TC))
    sbuf("bbc", (128, 2 * TC))
    sbuf("cbc", (128, 4 * TC))
    sbuf("ab", (128, 2 * TC)); sbuf("zb", (128, 2 * TC))
    sbuf("hb", (128, 2 * TC)); sbuf("mb", (128, 2 * TC))
    sbuf("state", (128, C.G * 8))
    sbuf("y2", (128, max(DIN_T * TC, 3 * 1024)))
    sbuf("macc", (128, KT * TC))
    sbuf("hid", (128, 8 * TC))
    sbuf("sg", (128, 2 * TC))
    sbuf("wsa", (128, KT * 128)); sbuf("wsb", (128, KT * 128))
    sbuf("wua", (128, KT * 128)); sbuf("wub", (128, KT * 128))
    sbuf("wda", (128, 8 * 128)); sbuf("wdb", (128, 8 * 128))
    sbuf("stats", (128, 16))
    sbuf("gcol", (128, 8 * max(C.KT, C.KT_FF) + 2))
    sbuf("srow", (1, 4 * TC))
    sbuf("a_sc", (128, C.G * 8))
    sbuf("ep", (128, 8 * 128))
    sbuf("sr", (128, 8 * 224))
    sbuf("ones_k", (128, 1)); sbuf("ones_1", (1, 128))
    sbuf("misc", (128, 8))
    sbuf("ccol", (128, 2 * DIN_T + 2 * KT))
    sbuf("thr", (128, 2))

    for i in range(8):
        ps[f"pb{i}"] = ctx.enter_context(nc.psum_tensor(f"pb{i}", [128, 512], F32))

    dbg = {}
    for name, key, shape in debug_outputs:
        dbg[name] = (nc.dram_tensor(name, list(shape), F32, kind="ExternalOutput"), key)

    def mm(bank_ap, stat_ap, mov_ap, start, stop, waits=()):
        return P.op("tensor",
                    lambda e: e.matmul(bank_ap(), stat_ap(), mov_ap(),
                                       start=start, stop=stop),
                    incs=([("pe", 1)] if stop else []), waits=waits)

    def act(out_ap, in_ap, func, bias=0.0, scale=1.0, accum=None, waits=()):
        def makef(e):
            kw = dict(bias=bias() if callable(bias) else bias,
                      scale=scale() if callable(scale) else scale)
            if accum is not None:
                kw["accum_out"] = accum()
            return e.activation(out=out_ap(), in_=in_ap(), func=func, **kw)
        return P.op("scalar", makef, incs=[("sc", 1)], waits=waits)

    def tt(out_ap, a_ap, b_ap, opx, waits=(), eng="vector", sem="dv"):
        return P.op(eng, lambda e: e.tensor_tensor(out_ap(), a_ap(), b_ap(), opx),
                    incs=[(sem, 1)], waits=waits)

    def stt(out_ap, in0, scal, in1, op0, op1, waits=(), eng="vector", sem="dv"):
        return P.op(eng,
                    lambda e: e.scalar_tensor_tensor(
                        out_ap(), in0(), scal() if callable(scal) else scal,
                        in1(), op0, op1),
                    incs=[(sem, 1)], waits=waits)

    def vcopy(out_ap, in_ap, waits=()):
        return P.op("vector", lambda e: e.tensor_copy(out_ap(), in_ap()),
                    incs=[("dv", 1)], waits=waits)

    def T_(name, t, w=None):
        w = TC if w is None else w
        return lambda name=name, t=t, w=w: sb[name][:, t * w:(t + 1) * w]

    def ccolap(off, i):
        return lambda off=off, i=i: sb["ccol"][:, off + i:off + i + 1]

    def miscap(i, p=128):
        return lambda i=i, p=p: sb["misc"][0:p, i:i + 1]

    def strip(name, s):
        return lambda name=name, s=s: sb[name][32 * (s % 4):32 * (s % 4) + 32,
                                               (s // 4) * TC:(s // 4) * TC + TC]

    cc_dtb, cc_D, cc_n1, cc_n2 = 0, DIN_T, 2 * DIN_T, 2 * DIN_T + KT

    P.dma("qs", lambda: sb["ones_k"][:], lambda: io["c_ones_k"].ap())
    P.dma("qs", lambda: sb["ones_1"][:], lambda: io["c_ones_1"].ap())
    P.dma("qs", lambda: sb["ep"][:], lambda: io["c_ep"].ap())
    P.dma("qs", lambda: sb["sr"][:], lambda: io["c_sr"].ap())
    P.dma("qs", lambda: sb["a_sc"][:], lambda: io["a_cols"].ap())
    P.dma("qs", lambda: sb["ccol"][:, cc_dtb:cc_dtb + DIN_T], lambda: io["c_dtb"].ap())
    P.dma("qs", lambda: sb["ccol"][:, cc_D:cc_D + DIN_T], lambda: io["c_D"].ap())
    P.dma("qs", lambda: sb["ccol"][:, cc_n1:cc_n1 + KT], lambda: io["c_n1w"].ap())
    P.dma("qs", lambda: sb["ccol"][:, cc_n2:cc_n2 + KT], lambda: io["c_n2w"].ap())
    P.dma("qs", lambda: sb["misc"][:], lambda: io["c_misc"].ap())
    tok_const = P.qtok("qs")
    act(lambda: sb["a_sc"][:], lambda: sb["a_sc"][:], AF.Exp, waits=[tok_const])
    P.op("vector", lambda e: e.tensor_scalar(sb["a_sc"][:], sb["a_sc"][:],
                                             -1.0, None, ALU.mult),
         incs=[("dv", 1)], waits=[P.tok("sc")])
    P.op("vector", lambda e: e.memset(sb["state"][:], 0.0), incs=[("dv", 1)])
    P.barrier()

    QW = 1024

    def qslice(which, w):
        return lambda which=which, w=w: sb["y2"][:, which * QW:which * QW + w]

    def quantize_weight(wname, qname, KS, OC):
        parts = [(p0, min(QW, OC - p0)) for p0 in range(0, OC, QW)]
        nslab = KS * len(parts)
        for k in range(KS):
            for hh, (p0, pw) in enumerate(parts):
                buf = (k * len(parts) + hh) % 2
                si = k * len(parts) + hh
                t_l = P.dma("qs", qslice(buf, pw),
                            lambda k=k, p0=p0, pw=pw: io[wname].ap()[k, :, p0:p0 + pw],
                            waits=([P.tok("sc")] if si >= 2 else
                                   [P.tok("dv"), P.qtok("qg")]))
                act(qslice(buf, pw), qslice(buf, pw),
                    AF.Abs, accum=lambda si=si: sb["gcol"][:, si:si + 1],
                    waits=[t_l] + ([P.tok("dv")] if si == 0 else []))
        P.op("vector", lambda e: e.reduce_sum(sb["stats"][:, 0:1],
                                              sb["gcol"][:, 0:nslab],
                                              axis=mybir.AxisListType.X),
             incs=[("dv", 1)], waits=[P.tok("sc"), P.tok("pe")])
        mm(lambda: ps["pb6"][0:1, 0:1], lambda: sb["ones_k"][:],
           lambda: sb["stats"][:, 0:1], True, True, waits=[P.tok("dv")])
        msl = 2 if OC == C.DFF else 4
        act(lambda: sb["stats"][0:1, 1:2], lambda: ps["pb6"][0:1, 0:1],
            AF.Identity, bias=miscap(3, 1), scale=miscap(msl, 1), waits=[P.tok("pe")])
        act(lambda: sb["stats"][0:1, 2:3], lambda: sb["stats"][0:1, 1:2],
            AF.Identity, scale=miscap(5, 1))
        mm(lambda: ps["pb7"][:, 0:1], lambda: sb["ones_1"][:],
           lambda: sb["stats"][0:1, 1:2], True, True, waits=[P.tok("sc")])
        a_t1 = act(lambda: sb["thr"][:, 0:1], lambda: ps["pb7"][:, 0:1],
                   AF.Identity, waits=[P.tok("pe"), P.tok("dv")])
        a_t2 = act(lambda: sb["thr"][:, 1:2], lambda: sb["thr"][:, 0:1],
                   AF.Identity, scale=miscap(5))
        tok_thr = a_t2
        for k in range(KS):
            for hh, (p0, pw) in enumerate(parts):
                buf = (k * len(parts) + hh) % 2
                t_l = P.dma("qs", qslice(buf, pw),
                            lambda k=k, p0=p0, pw=pw: io[wname].ap()[k, :, p0:p0 + pw],
                            waits=[P.qtok("qg"), P.tok("sc")])
                t2 = P.op("vector", lambda e, b=buf, pw=pw: e.tensor_scalar(
                    sb["y2"][:, 2 * QW:2 * QW + pw], qslice(b, pw)(), sb["thr"][:, 1:2],
                    None, ALU.is_le),
                    incs=[("dv", 1)], waits=[t_l, tok_thr, P.tok("dv")])
                qv = P.op("vector", lambda e, b=buf, pw=pw: e.scalar_tensor_tensor(
                    qslice(b, pw)(), qslice(b, pw)(), sb["thr"][:, 0:1],
                    sb["y2"][:, 2 * QW:2 * QW + pw], ALU.is_ge, ALU.subtract),
                    incs=[("dv", 1)])
                for oo in range(pw // 128):
                    o_glob = p0 // 128 + oo
                    P.dma("qg", lambda k=k, o=o_glob, q=qname: qscr[q].ap()[o, k],
                          lambda b=buf, oo=oo: sb["y2"][:, b * QW + oo * 128:b * QW + (oo + 1) * 128],
                          waits=[qv])

    quantize_weight("w_f1", "q1", KT, C.DFF)
    quantize_weight("w_f2", "q2", KT, C.DFF)
    quantize_weight("w_f3", "q3", C.KT_FF, C.DM)
    P.barrier()

    def emit_norm(src_tiles, dst_tiles, kind, nw_off=None, sub_mean=True):
        for k in range(KT):
            a1 = act(T_("sq", k % 2), src_tiles[k], AF.Square,
                     waits=[P.tok("pe")] if k >= 2 else [])
            P.op("tensor", lambda e, k=k: e.matmul(
                ps["pb4"][0:1, 0:TC], sb["ones_k"][:], sb["sq"][:, (k % 2) * TC:(k % 2) * TC + TC],
                start=(k == 0), stop=(k == KT - 1)),
                incs=[("pe", 1)], waits=[a1])
            if kind == "ln":
                mm(lambda: ps["pb5"][0:1, 0:TC], lambda: sb["ones_k"][:],
                   src_tiles[k], k == 0, k == KT - 1)
        tok_mm = P.tok("pe")
        DMf = float(C.DM)
        if kind == "ln":
            a_m = act(lambda: sb["srow"][0:1, 0:TC], lambda: ps["pb5"][0:1, 0:TC],
                      AF.Identity, scale=miscap(0, 1), waits=[tok_mm])
            act(lambda: sb["srow"][0:1, TC:2 * TC], lambda: sb["srow"][0:1, 0:TC],
                AF.Square)
            a_e = act(lambda: sb["srow"][0:1, 2 * TC:3 * TC],
                      lambda: ps["pb4"][0:1, 0:TC], AF.Identity, scale=miscap(0, 1))
            v_var = tt(lambda: sb["srow"][0:1, TC:2 * TC],
                       lambda: sb["srow"][0:1, 2 * TC:3 * TC],
                       lambda: sb["srow"][0:1, TC:2 * TC], ALU.subtract,
                       waits=[a_e])
            a_sd = act(lambda: sb["srow"][0:1, TC:2 * TC],
                       lambda: sb["srow"][0:1, TC:2 * TC], AF.Sqrt, bias=miscap(1, 1),
                       waits=[v_var])
        else:
            a_sd = act(lambda: sb["srow"][0:1, TC:2 * TC],
                       lambda: ps["pb4"][0:1, 0:TC], AF.Sqrt, bias=miscap(1, 1),
                       scale=miscap(0, 1), waits=[tok_mm])
        r_rc = P.op("vector", lambda e: e.reciprocal(
            sb["srow"][0:1, 3 * TC:4 * TC], sb["srow"][0:1, TC:2 * TC]),
            incs=[("dv", 1)], waits=[a_sd])
        mm(lambda: ps["pb4"][:, 0:TC], lambda: sb["ones_1"][:],
           lambda: sb["srow"][0:1, 3 * TC:4 * TC], True, True, waits=[r_rc])
        if kind == "ln":
            mm(lambda: ps["pb5"][:, 0:TC], lambda: sb["ones_1"][:],
               lambda: sb["srow"][0:1, 0:TC], True, True, waits=[a_m])
        tok_bc = P.tok("pe")
        for k in range(KT):
            if kind == "ln":
                tt(dst_tiles[k], src_tiles[k], lambda: ps["pb5"][:, 0:TC],
                   ALU.subtract, waits=[tok_bc])
                tt(dst_tiles[k], dst_tiles[k], lambda: ps["pb4"][:, 0:TC], ALU.mult)
            else:
                stt(dst_tiles[k], src_tiles[k], ccolap(nw_off, k),
                    lambda: ps["pb4"][:, 0:TC], ALU.mult, ALU.mult, waits=[tok_bc])

    def stream_proj(wname, o_list, src_tiles, nk, evac, qdram=None):
        wb = ("wsa", "wsb")
        wtok = [None, None]
        btok = [None, None]
        for idx, o in enumerate(o_list):
            pbuf = wb[idx % 2]
            bank = f"pb{idx % 2}"
            src_dram = (lambda o=o, w=wname: io[w].ap()[o].rearrange("k r c -> r k c")) \
                if qdram is None \
                else (lambda o=o, q=qdram: qscr[q].ap()[o].rearrange("k r c -> r k c"))
            t_w = P.dma("qs", lambda b=pbuf, nk=nk: sb[b][:, 0:nk * 128], src_dram,
                        waits=[wtok[idx % 2]] if wtok[idx % 2] else [])
            for k in range(nk):
                mm(lambda bank=bank: ps[bank][:, 0:TC],
                   lambda b=pbuf, k=k: sb[b][:, k * 128:(k + 1) * 128],
                   src_tiles[k], k == 0, k == nk - 1,
                   waits=([t_w, btok[idx % 2]] if k == 0 else []))
            wtok[idx % 2] = P.tok("pe")
            btok[idx % 2] = evac(o, lambda bank=bank: ps[bank][:, 0:TC], P.tok("pe"))

    npre = C.NCH if n_pre_chunks is None else n_pre_chunks
    chunks = [("pre", i) for i in range(npre)] + [("own", i) for i in range(C.NCH)]

    for ckind, ci in chunks:
        own = ckind == "own"
        xsrc = "x_own" if own else "x_pre"
        c0 = ci * TC

        for k in range(KT):
            P.dma("qs", T_("xb", k),
                  lambda k=k, c0=c0, xs=xsrc: io[xs].ap()[k * 128:(k + 1) * 128,
                                                          c0:c0 + TC])
        P.barrier()
        emit_norm([T_("xb", k) for k in range(KT)],
                  [T_("ln1", k) for k in range(KT)], "ln")
        P.barrier()
        emit_norm([T_("ln1", k) for k in range(KT)],
                  [T_("u", k) for k in range(KT)], "rms", nw_off=cc_n1)
        P.barrier()

        o_order = [C.o_dt] + list(range(C.o_x0, C.o_c0))

        def evac_in(o, bank_ap, tok):
            if o == C.o_dt:
                return vcopy(lambda: sb["dtraw"][:, 0:TC], bank_ap, waits=[tok])
            if C.o_x0 <= o < C.o_b0:
                return vcopy(T_("xv", o - C.o_x0), bank_ap, waits=[tok])
            return vcopy(T_("bv", o - C.o_b0), bank_ap, waits=[tok])

        stream_proj("w_in", o_order, [T_("u", k) for k in range(KT)], KT, evac_in)
        P.barrier()

        def proj_tile(wname, o, src_tiles, nk, bank, qdram=None, extra_waits=()):
            src_dram = (lambda o=o, w=wname: io[w].ap()[o].rearrange("k r c -> r k c"))
            t_w = P.dma("qs", lambda nk=nk: sb["wsa"][:, 0:nk * 128], src_dram,
                        waits=[P.tok("pe")])
            for k in range(nk):
                mm(lambda bank=bank: ps[bank][:, 0:TC],
                   lambda k=k: sb["wsa"][:, k * 128:(k + 1) * 128],
                   src_tiles[k], k == 0, k == nk - 1,
                   waits=([t_w] + list(extra_waits) if k == 0 else []))
            return P.tok("pe")

        for q in range(NQ):
            r2 = q % 2
            tok_dtq = proj_tile("w_dt", q, [lambda: sb["dtraw"][:, 0:TC]], 1, "pb1",
                                extra_waits=[P.tok("sc")])
            act(T_("dt", r2), lambda: ps["pb1"][:, 0:TC], AF.Exp,
                bias=ccolap(cc_dtb, q), waits=[tok_dtq, P.tok("pe")])
            a_dt = act(T_("dt", r2), T_("dt", r2), AF.Ln, bias=1.0)
            for r in range(4):
                P.dma("qg", lambda r=r, r2=r2: sb["xrep"][32 * r:32 * r + 32,
                                                          r2 * TC:r2 * TC + TC],
                      strip("xv", q), waits=[P.tok("dv")] if r == 0 else [])
            tok_xr = P.qtok("qg")
            v_dtx = tt(T_("dtx", r2), T_("dt", r2), T_("xrep", r2), ALU.mult,
                       waits=[a_dt, tok_xr])
            for r in range(4):
                P.dma("qg", lambda q=q, r=r: sb["bbc"][32 * r:32 * r + 32,
                                                       (q % 2) * TC:(q % 2) * TC + TC],
                      strip("bv", q), waits=[P.tok("dv")] if r == 0 else [])
            if own:
                tok_z = proj_tile("w_in", q, [T_("u", k) for k in range(KT)], KT,
                                  "pb2", extra_waits=[P.tok("sc")])
                act(T_("zs", r2), lambda: ps["pb2"][:, 0:TC], AF.Silu, waits=[tok_z])
                tok_c = proj_tile("w_in", C.o_c0 + q, [T_("u", k) for k in range(KT)],
                                  KT, "pb3", extra_waits=[P.tok("dv")])
                v_c = vcopy(T_("cv", r2), lambda: ps["pb3"][:, 0:TC], waits=[tok_c])
                for gg in range(4):
                    for r in range(4):
                        P.dma("qg", lambda gg=gg, r=r: sb["cbc"][32 * r:32 * r + 32,
                                                                 gg * TC:gg * TC + TC],
                              lambda gg=gg, r2=r2: sb["cv"][32 * gg:32 * gg + 32,
                                                            r2 * TC:r2 * TC + TC],
                              waits=[P.tok("gp"), v_c] if (gg == 0 and r == 0) else [])
            tok_cbc = P.qtok("qg")
            y_open = False
            for j in range(8):
                for gp in range(2):
                    toks_rep = []
                    for r in range(2):
                        g = 4 * q + 2 * gp + r
                        bA, bB = f"pb{4 + 2 * r}", f"pb{5 + 2 * r}"
                        sp = 32 * (g % 4)
                        epj = lambda j=j, sp=sp: sb["ep"][sp:sp + 32,
                                                          j * 128:(j + 1) * 128]
                        dtrows = lambda sp=sp, r2=r2: sb["dt"][sp:sp + 32,
                                                      r2 * TC:r2 * TC + TC]
                        dtxrows = lambda sp=sp, r2=r2: sb["dtx"][sp:sp + 32,
                                                        r2 * TC:r2 * TC + TC]
                        tp = (sp, 0)
                        P.op("tensor", lambda e, bA=bA, epj=epj, dtrows=dtrows, tp=tp:
                             e.matmul(ps[bA][:, 0:TC], epj(), dtrows(), start=True,
                                      stop=True, tile_position=tp),
                             incs=[("pe", 1)], waits=[P.tok("sc"), P.tok("dv"), v_dtx])
                        P.op("tensor", lambda e, bB=bB, epj=epj, dtxrows=dtxrows, tp=tp:
                             e.matmul(ps[bB][:, 0:TC], epj(), dtxrows(), start=True,
                                      stop=True, tile_position=tp),
                             incs=[("pe", 1)])
                        toks_rep.append((g, bA, bB, P.tok("pe")))
                    for ri, (g, bA, bB, tok_r) in enumerate(toks_rep):
                        a1 = act(T_("ab", ri), lambda bA=bA: ps[bA][:, 0:TC], AF.Exp,
                                 scale=lambda g=g, j=j: sb["a_sc"][:, 8 * g + j:8 * g + j + 1],
                                 waits=[tok_r])
                        z1 = tt(T_("zb", ri), lambda bB=bB: ps[bB][:, 0:TC],
                                T_("bbc", q % 2), ALU.mult, waits=[tok_r, tok_cbc])
                        h1 = P.op("vector", lambda e, ri=ri, g=g, j=j: e.tensor_tensor_scan(
                            sb["hb"][:, ri * TC:ri * TC + TC],
                            sb["ab"][:, ri * TC:ri * TC + TC],
                            sb["zb"][:, ri * TC:ri * TC + TC],
                            sb["state"][:, 8 * g + j:8 * g + j + 1],
                            ALU.mult, ALU.add),
                            incs=[("dv", 1)], waits=[a1, z1])
                        vcopy(lambda g=g, j=j: sb["state"][:, 8 * g + j:8 * g + j + 1],
                              lambda ri=ri: sb["hb"][:, ri * TC + TC - 1:ri * TC + TC])
                        if own:
                            mmul = tt(T_("mb", ri), T_("hb", ri),
                                      T_("cbc", g - 4 * q), ALU.mult,
                                      waits=[h1, tok_cbc], eng="gpsimd", sem="gp")
                            rr = g - 4 * q
                            off = j * 224 + 96 - 32 * rr
                            mm(lambda: ps["pb0"][:, 0:TC],
                               lambda off=off: sb["sr"][:, off:off + 128],
                               T_("mb", ri), not y_open,
                               (j == 7 and gp == 1 and ri == 1), waits=[mmul])
                            y_open = True
            if own:
                tok_y = P.tok("pe")
                stt(T_("y2", q), T_("xrep", r2), ccolap(cc_D, q),
                    lambda: ps["pb0"][:, 0:TC], ALU.mult, ALU.add,
                    waits=[tok_y, P.tok("sc")])
                tt(T_("y2", q), T_("y2", q), T_("zs", r2), ALU.mult)
        P.barrier()

        if not own:
            continue

        def evac_o(o, bank_ap, tok):
            return tt(T_("ln1", o), T_("ln1", o), bank_ap, ALU.add, waits=[tok])

        stream_proj("w_o", list(range(KT)), [T_("y2", k) for k in range(DIN_T)],
                    DIN_T, evac_o)
        P.barrier()

        emit_norm([T_("ln1", k) for k in range(KT)],
                  [T_("u", k) for k in range(KT)], "rms", nw_off=cc_n2)
        for k in range(KT):
            tt(T_("macc", k), T_("xb", k), T_("ln1", k), ALU.add)
        P.barrier()

        def gated_ffn(wg, wu, wd, actf, src_tiles, acc_tiles, qn=None):
            HB = 8
            nhb = (C.OT_FF + HB - 1) // HB
            gb, ub, db = ("wsa", "wsb"), ("wua", "wub"), ("wda", "wdb")
            wtok = [None, None]
            utok = [None, None]
            dtok = [None, None]
            hbank = [None, None]
            for hb in range(nhb):
                o_lo, o_hi = hb * HB, min(C.OT_FF, hb * HB + HB)
                for o in range(o_lo, o_hi):
                    par = o % 2
                    bA, bB = f"pb{2 + par}", f"pb{4 + par}"
                    if qn is None:
                        srcg = lambda o=o, w=wg: io[w].ap()[o].rearrange("k r c -> r k c")
                        srcu = lambda o=o, w=wu: io[w].ap()[o].rearrange("k r c -> r k c")
                    else:
                        srcg = lambda o=o: qscr[qn[0]].ap()[o].rearrange("k r c -> r k c")
                        srcu = lambda o=o: qscr[qn[1]].ap()[o].rearrange("k r c -> r k c")
                    t_wg = P.dma("qs", lambda b=gb[par]: sb[b][:, 0:KT * 128], srcg,
                                 waits=[wtok[par]] if wtok[par] else [])
                    t_wu = P.dma("qs", lambda b=ub[par]: sb[b][:, 0:KT * 128], srcu,
                                 waits=[utok[par]] if utok[par] else [])
                    for k in range(KT):
                        mm(lambda bA=bA: ps[bA][:, 0:TC],
                           lambda b=gb[par], k=k: sb[b][:, k * 128:(k + 1) * 128],
                           src_tiles[k], k == 0, k == KT - 1,
                           waits=([t_wg, hbank[par]] if k == 0 else []))
                    wtok[par] = P.tok("pe")
                    for k in range(KT):
                        mm(lambda bB=bB: ps[bB][:, 0:TC],
                           lambda b=ub[par], k=k: sb[b][:, k * 128:(k + 1) * 128],
                           src_tiles[k], k == 0, k == KT - 1,
                           waits=([t_wu, hbank[par]] if k == 0 else []))
                    utok[par] = P.tok("pe")
                    a1 = act(T_("sg", par), lambda bA=bA: ps[bA][:, 0:TC], actf,
                             waits=[utok[par]])
                    hbank[par] = tt(T_("hid", o - o_lo), T_("sg", par),
                                    lambda bB=bB: ps[bB][:, 0:TC], ALU.mult,
                                    waits=[a1])
                tok_hid = P.tok("dv")
                nk3 = o_hi - o_lo
                for o2 in range(KT):
                    par = o2 % 2
                    bank = f"pb{par}"
                    if qn is None:
                        src3 = lambda o2=o2, lo=o_lo, hi=o_hi, w=wd: io[w].ap()[o2, lo:hi].rearrange("k r c -> r k c")
                    else:
                        src3 = lambda o2=o2, lo=o_lo, hi=o_hi: qscr[qn[2]].ap()[o2, lo:hi].rearrange("k r c -> r k c")
                    t_w3 = P.dma("qs", lambda b=db[par], nk3=nk3: sb[b][:, 0:nk3 * 128],
                                 src3, waits=[dtok[par]] if dtok[par] else [])
                    for kk in range(nk3):
                        mm(lambda bank=bank: ps[bank][:, 0:TC],
                           lambda b=db[par], kk=kk: sb[b][:, kk * 128:(kk + 1) * 128],
                           T_("hid", kk), kk == 0, kk == nk3 - 1,
                           waits=([t_w3, tok_hid, dtok[par]] if kk == 0 else []))
                    dtok[par] = tt(acc_tiles[o2], acc_tiles[o2],
                                   lambda bank=bank: ps[bank][:, 0:TC], ALU.add,
                                   waits=[P.tok("pe")])
                P.barrier()

        gated_ffn("w_g", "w_u", "w_d", AF.Silu,
                  [T_("u", k) for k in range(KT)],
                  [T_("macc", k) for k in range(KT)])
        P.barrier()

        emit_norm([T_("macc", k) for k in range(KT)],
                  [T_("ln1", k) for k in range(KT)], "ln")
        for k in range(KT):
            vcopy(T_("y2", k), T_("macc", k))
        P.barrier()
        gated_ffn(None, None, None, AF.Sigmoid,
                  [T_("ln1", k) for k in range(KT)],
                  [T_("y2", k) for k in range(KT)], qn=("q1", "q2", "q3"))
        P.barrier()

        for k in range(KT):
            P.dma("qg", lambda k=k, c0=c0: io["y_out"].ap()[k * 128:(k + 1) * 128,
                                                            c0:c0 + TC],
                  T_("y2", k))
        P.barrier()

    for name, (dr, key) in dbg.items():
        P.dma("qg", lambda dr=dr: dr.ap(), lambda key=key: sb[key][:])
    P.wait("gpsimd", P.qtok("qg"), P.qtok("qs"), P.qtok("qa"))

    with ctx:
        with ExitStack() as sctx:
            sems = {s: sctx.enter_context(nc.semaphore(s)) for s in ESEMS}
            with nc.Block() as block:
                @block.sync
                def _(e):
                    for fn in P.sched["sync"]:
                        fn(e, sems)

                @block.tensor
                def _(e):
                    for fn in P.sched["tensor"]:
                        fn(e, sems)

                @block.vector
                def _(e):
                    for fn in P.sched["vector"]:
                        fn(e, sems)

                @block.scalar
                def _(e):
                    for fn in P.sched["scalar"]:
                        fn(e, sems)

                @block.gpsimd
                def _(e):
                    for fn in P.sched["gpsimd"]:
                        fn(e, sems)
    return nc



def kernel(**inputs):
    C = CFG()
    nc = build(C)
    wts = host_weights(C, inputs)
    consts = host_constants(C)
    in_maps = []
    for core in range(8):
        m = dict(wts)
        m.update(consts)
        m.update(host_core_inputs(C, inputs, core, 2048))
        in_maps.append(m)
    res = run_bass_kernel_spmd(nc, in_maps, core_ids=list(range(8)))
    out = np.empty((4, 2048, 2048), np.float32)
    for core in range(8):
        b, half = divmod(core, 2)
        y = np.asarray(res.results[core]["y_out"])
        out[b, C.TOWN * half:C.TOWN * (half + 1), :] = y.T
    return out


# version 02 (final)
# speedup vs baseline: 1.0599x; 1.0599x over previous
import numpy as np
import ml_dtypes
from contextlib import ExitStack

import concourse.bass as bass
import concourse.mybir as mybir
from concourse.bass_utils import run_bass_kernel_spmd

F32 = mybir.dt.float32
BF16 = mybir.dt.bfloat16
AF = mybir.ActivationFunctionType
ALU = mybir.AluOpType

EPS_LN = 1e-5
EPS_RMS = 1e-5
EPS_Q = 1e-8


class CFG:
    def __init__(self, DM=2048, DIN=2048, DXB=512, DTR=128, DFF=5632,
                 TOWN=1024, TC=256):
        self.DM, self.DIN, self.DXB, self.DTR, self.DFF = DM, DIN, DXB, DTR, DFF
        self.TOWN, self.TC = TOWN, TC
        self.G = DIN // 32
        self.GKV = DXB // 32
        assert self.G == 4 * self.GKV
        self.KT = DM // 128
        self.PROJ = 2 * DIN + 2 * DXB + DTR
        self.OT_IN = self.PROJ // 128
        self.OT_Z = DIN // 128
        self.OT_X = DXB // 128
        self.OT_FF = DFF // 128
        self.KT_FF = DFF // 128
        self.DIN_T = DIN // 128
        self.NQ = self.G // 4
        self.NCH = TOWN // TC
        self.o_x0 = self.OT_Z
        self.o_b0 = self.OT_Z + self.OT_X
        self.o_c0 = self.OT_Z + 2 * self.OT_X
        self.o_dt = self.OT_IN - 1
        assert DTR == 128 and self.o_dt == self.o_c0 + self.DIN_T



def tile_w(wT, K, O):
    kt, ot = K // 128, O // 128
    return np.ascontiguousarray(
        np.asarray(wT, np.float32).reshape(kt, 128, ot, 128).transpose(2, 0, 1, 3))


def host_constants(C):
    d = {}
    ep = np.zeros((8, 32, 128), np.float32)
    for j in range(8):
        for r in range(128):
            ep[j, 4 * j + r // 32, r] = 1.0
    ep2 = np.ascontiguousarray(ep.transpose(1, 0, 2).reshape(32, 8 * 128))
    d["c_ep"] = np.ascontiguousarray(np.tile(ep2, (4, 1))).astype(ml_dtypes.bfloat16)
    srp = np.zeros((8, 128, 224), np.float32)
    for j in range(8):
        for k in range(128):
            srp[j, k, 96 + 4 * j + k // 32] = 1.0
    d["c_sr"] = np.ascontiguousarray(srp.transpose(1, 0, 2).reshape(128, 8 * 224))
    d["c_ones_k"] = np.ones((128, 1), np.float32)
    d["c_ones_1"] = np.ones((1, 128), np.float32)
    misc = np.zeros((128, 8), np.float32)
    misc[:, 0] = 1.0 / C.DM
    misc[:, 1] = EPS_LN
    misc[:, 2] = 0.5 / (C.DM * C.DFF)
    misc[:, 3] = 0.5 * EPS_Q
    misc[:, 4] = 0.5 / (C.KT_FF * 128 * C.DM)
    misc[:, 5] = -1.0
    d["c_misc"] = misc
    return d


def host_weights(C, inputs):
    d = {}
    d["w_in"] = tile_w(np.asarray(inputs["L0_attn_in_proj_w"]).T, C.DM, C.PROJ)
    d["w_dt"] = tile_w(np.asarray(inputs["L0_attn_dt_proj_w"]).T, C.DTR, C.DIN)
    d["w_o"] = tile_w(np.asarray(inputs["L0_attn_out_proj_w"]).T, C.DIN, C.DM)
    d["w_g"] = tile_w(np.asarray(inputs["L0_attn_gate_w"]).T, C.DM, C.DFF)
    d["w_u"] = tile_w(np.asarray(inputs["L0_attn_up_w"]).T, C.DM, C.DFF)
    d["w_d"] = tile_w(np.asarray(inputs["L0_attn_down_w"]).T, C.DFF, C.DM)
    d["w_f1"] = np.ascontiguousarray(
        np.asarray(inputs["L0_ffn_w1"], np.float32).T.reshape(C.KT, 128, C.DFF))
    d["w_f2"] = np.ascontiguousarray(
        np.asarray(inputs["L0_ffn_w2"], np.float32).T.reshape(C.KT, 128, C.DFF))
    d["w_f3"] = np.ascontiguousarray(
        np.asarray(inputs["L0_ffn_w3"], np.float32).T.reshape(C.KT_FF, 128, C.DM))
    alog = np.asarray(inputs["L0_attn_A_log"], np.float32)
    ac = np.zeros((128, C.G * 8), np.float32)
    for g in range(C.G):
        for j in range(8):
            for r in range(128):
                ac[r, g * 8 + j] = alog[32 * g + 4 * j + r // 32, r % 32]
    d["a_cols"] = ac

    def colT(v, n):
        return np.ascontiguousarray(np.asarray(v, np.float32).reshape(n // 128, 128).T)
    d["c_dtb"] = colT(inputs["L0_attn_dt_proj_b"], C.DIN)
    d["c_D"] = colT(inputs["L0_attn_D_param"], C.DIN)
    d["c_n1w"] = colT(inputs["L0_attn_norm1_w"], C.DM)
    d["c_n2w"] = colT(inputs["L0_attn_norm2_w"], C.DM)
    return d


def host_core_inputs(C, inputs, core):
    b, half = divmod(core, 2)
    x = np.asarray(inputs["L0_attn_hidden_states"], np.float32)
    xT = np.ascontiguousarray(x[b].T)
    t0 = C.TOWN * half
    return {
        "x_own": np.ascontiguousarray(xT[:, t0:t0 + C.TOWN]),
        "x_pre": (np.ascontiguousarray(xT[:, 0:C.TOWN]) if half == 1
                  else np.zeros((C.DM, C.TOWN), np.float32)),
    }



ENGINES = ("sync", "tensor", "vector", "scalar", "gpsimd")
ESEMS = ("pe", "dv", "sc", "gp", "qs", "qg", "qa")


class Cell:
    __slots__ = ("v",)

    def __init__(self):
        self.v = None


QENG = {"qs": "sync", "qg": "gpsimd", "qa": "scalar"}
FENCE_EVERY = {"qs": 2, "qg": 8, "qa": 2}


def _rv(x):
    return x.v if isinstance(x, Cell) else x


class Prog:
    def __init__(self):
        self.sched = {e: [] for e in ENGINES}
        self.cnt = {s: 0 for s in ESEMS}
        self.pending = {q: [] for q in QENG}

    def tok(self, sem):
        return (sem, self.cnt[sem])

    def emit(self, eng, fn):
        self.sched[eng].append(fn)

    OWN = {"vector": "dv", "scalar": "sc", "gpsimd": "gp"}

    def op(self, eng, make, incs=(), waits=()):
        waits = [w for w in waits if w is not None]
        own = self.OWN.get(eng)
        if own is not None and self.cnt[own] > 0 and any(s == own for s, _ in incs):
            waits.append((own, self.cnt[own]))
        targets = []
        for (s, n) in incs:
            self.cnt[s] += n
            targets.append((s, self.cnt[s]))

        def fn(e, sems):
            for (s, v) in waits:
                v = _rv(v)
                if v:
                    e.wait_ge(sems[s], v)
            ins = make(e)
            for (s, v) in targets:
                ins.then_inc(sems[s], 16 if s.startswith("q") else 1)
        self.emit(eng, fn)
        return targets[0] if targets else None

    def wait(self, eng, *toks):
        toks = [t for t in toks if t is not None]
        if not toks:
            return

        def fn(e, sems):
            for (s, v) in toks:
                v = _rv(v)
                if v:
                    e.wait_ge(sems[s], v)
        self.emit(eng, fn)

    def fence(self, queue):
        if not self.pending[queue]:
            return
        total = self.cnt[queue]
        for c in self.pending[queue]:
            c.v = total
        self.pending[queue].clear()
        self.wait(QENG[queue], (queue, total))

    def dma(self, queue, out_ap, in_ap, waits=()):
        if any(w is not None for w in waits):
            self.fence(queue)
        tok = self.op(QENG[queue], lambda e: e.dma_start(out=out_ap(), in_=in_ap()),
                      incs=[(queue, 16)], waits=waits)
        cell = Cell()
        self.pending[queue].append(cell)
        if len(self.pending[queue]) >= FENCE_EVERY[queue]:
            self.fence(queue)
        return (queue, cell)

    def qtok(self, queue):
        self.fence(queue)
        return (queue, self.cnt[queue])

    def barrier(self):
        for q in QENG:
            self.fence(q)
        toks = [self.tok(s) for s in ESEMS]
        for e in ENGINES:
            self.wait(e, *toks)


def build(C, debug_outputs=(), n_pre_chunks=None):
    nc = bass.Bass("TRN2", target_bir_lowering=False, debug=False)
    P = Prog()
    TC, KT, DIN_T, NQ = C.TC, C.KT, C.DIN_T, C.NQ
    io = {}

    def din(name, shape, dtype=F32):
        io[name] = nc.dram_tensor(name, list(shape), dtype, kind="ExternalInput")

    def dout(name, shape):
        io[name] = nc.dram_tensor(name, list(shape), F32, kind="ExternalOutput")

    din("x_own", (C.DM, C.TOWN)); din("x_pre", (C.DM, C.TOWN))
    din("w_in", (C.OT_IN, KT, 128, 128)); din("w_dt", (DIN_T, 1, 128, 128))
    din("w_o", (KT, DIN_T, 128, 128))
    din("w_g", (C.OT_FF, KT, 128, 128)); din("w_u", (C.OT_FF, KT, 128, 128))
    din("w_d", (KT, C.KT_FF, 128, 128))
    din("w_f1", (KT, 128, C.DFF)); din("w_f2", (KT, 128, C.DFF))
    din("w_f3", (C.KT_FF, 128, C.DM))
    din("a_cols", (128, C.G * 8))
    din("c_dtb", (128, DIN_T)); din("c_D", (128, DIN_T))
    din("c_n1w", (128, KT)); din("c_n2w", (128, KT))
    din("c_ep", (128, 8 * 128), BF16); din("c_sr", (128, 8 * 224))
    din("c_ones_k", (128, 1)); din("c_ones_1", (1, 128))
    din("c_misc", (128, 8))
    dout("y_out", (C.DM, C.TOWN))
    qscr = {
        "q1": nc.dram_tensor("q1s", [C.OT_FF, KT, 128, 128], BF16, kind="Internal"),
        "q2": nc.dram_tensor("q2s", [C.OT_FF, KT, 128, 128], BF16, kind="Internal"),
        "q3": nc.dram_tensor("q3s", [KT, C.KT_FF, 128, 128], BF16, kind="Internal"),
    }
    wshapes = {"w_in": (C.OT_IN, KT), "w_dt": (DIN_T, 1), "w_o": (KT, DIN_T),
               "w_g": (C.OT_FF, KT), "w_u": (C.OT_FF, KT), "w_d": (KT, C.KT_FF)}
    wcache = {}
    for wn, (O, K) in wshapes.items():
        wcache[wn] = (
            nc.dram_tensor(wn + "_ch", [O, K, 128, 128], BF16, kind="Internal"),
            nc.dram_tensor(wn + "_cl", [O, K, 128, 128], BF16, kind="Internal"))
    cached = {}

    ctx = ExitStack()
    sb, ps = {}, {}

    def sbuf(name, shape, dtype=F32):
        sb[name] = ctx.enter_context(nc.sbuf_tensor(name, list(shape), dtype))

    sbuf("xb", (128, KT * TC))
    sbuf("sq", (128, 2 * TC))
    sbuf("ln1", (128, KT * TC))
    sbuf("u", (128, 2 * TC))
    sbuf("sh", (128, KT * TC), BF16)
    sbuf("sl", (128, KT * TC), BF16)
    sbuf("zs", (128, 2 * TC))
    sbuf("xv", (128, C.OT_X * TC))
    sbuf("bv", (128, C.OT_X * TC))
    sbuf("cv", (128, 2 * TC))
    sbuf("dtraw", (128, TC))
    sbuf("dtrh", (128, TC), BF16); sbuf("dtrl", (128, TC), BF16)
    sbuf("dt", (128, 2 * TC))
    sbuf("dth", (128, 2 * TC), BF16); sbuf("dtl", (128, 2 * TC), BF16)
    sbuf("dtx", (128, 2 * TC))
    sbuf("dtxh", (128, 2 * TC), BF16); sbuf("dtxl", (128, 2 * TC), BF16)
    sbuf("xrep", (128, 2 * TC))
    sbuf("bbc", (128, 2 * TC))
    sbuf("cbc", (128, 4 * TC))
    sbuf("ab", (128, 2 * TC)); sbuf("zb", (128, 2 * TC))
    sbuf("hb", (128, 2 * TC)); sbuf("mb", (128, 2 * TC))
    sbuf("state", (128, C.G * 8))
    sbuf("y2", (128, 2 * TC))
    sbuf("yh", (128, DIN_T * TC), BF16); sbuf("yl", (128, DIN_T * TC), BF16)
    sbuf("macc", (128, max(KT * TC, 3 * 1024)))
    sbuf("hid", (128, 2 * TC))
    sbuf("hh2", (128, 8 * TC), BF16); sbuf("hl2", (128, 8 * TC), BF16)
    sbuf("sg", (128, 2 * TC))
    sbuf("wf32", (128, KT * 128))
    sbuf("whA", (128, KT * 128), BF16); sbuf("wlA", (128, KT * 128), BF16)
    sbuf("whB", (128, KT * 128), BF16); sbuf("wlB", (128, KT * 128), BF16)
    sbuf("whU", (128, KT * 128), BF16); sbuf("wlU", (128, KT * 128), BF16)
    sbuf("whV", (128, KT * 128), BF16); sbuf("wlV", (128, KT * 128), BF16)
    sbuf("whD", (128, 8 * 128), BF16); sbuf("wlD", (128, 8 * 128), BF16)
    sbuf("whE", (128, 8 * 128), BF16); sbuf("wlE", (128, 8 * 128), BF16)
    sbuf("stats", (128, 16))
    sbuf("gcol", (128, 8 * max(C.KT, C.KT_FF) + 2))
    sbuf("srow", (1, 4 * TC))
    sbuf("a_sc", (128, C.G * 8))
    sbuf("ep", (128, 8 * 128), BF16)
    sbuf("sr", (128, 8 * 224))
    sbuf("ones_k", (128, 1)); sbuf("ones_1", (1, 128))
    sbuf("misc", (128, 8))
    sbuf("ccol", (128, 2 * DIN_T + 2 * KT))
    sbuf("thr", (128, 2))

    for i in range(8):
        ps[f"pb{i}"] = ctx.enter_context(nc.psum_tensor(f"pb{i}", [128, 512], F32))

    dbg = {}
    for name, key, shape in debug_outputs:
        dbg[name] = (nc.dram_tensor(name, list(shape), F32, kind="ExternalOutput"), key)

    def mm(bank_ap, stat_ap, mov_ap, start, stop, waits=()):
        return P.op("tensor",
                    lambda e: e.matmul(bank_ap(), stat_ap(), mov_ap(),
                                       start=start, stop=stop),
                    incs=([("pe", 1)] if stop else []), waits=waits)

    def act(out_ap, in_ap, func, bias=0.0, scale=1.0, accum=None, waits=()):
        def makef(e):
            kw = dict(bias=bias() if callable(bias) else bias,
                      scale=scale() if callable(scale) else scale)
            if accum is not None:
                kw["accum_out"] = accum()
            return e.activation(out=out_ap(), in_=in_ap(), func=func, **kw)
        return P.op("scalar", makef, incs=[("sc", 1)], waits=waits)

    def tt(out_ap, a_ap, b_ap, opx, waits=(), eng="vector", sem="dv"):
        return P.op(eng, lambda e: e.tensor_tensor(out_ap(), a_ap(), b_ap(), opx),
                    incs=[(sem, 1)], waits=waits)

    def stt(out_ap, in0, scal, in1, op0, op1, waits=(), eng="vector", sem="dv"):
        return P.op(eng,
                    lambda e: e.scalar_tensor_tensor(
                        out_ap(), in0(), scal() if callable(scal) else scal,
                        in1(), op0, op1),
                    incs=[(sem, 1)], waits=waits)

    def vcopy(out_ap, in_ap, waits=()):
        return P.op("vector", lambda e: e.tensor_copy(out_ap(), in_ap()),
                    incs=[("dv", 1)], waits=waits)

    def split(h_ap, l_ap, src_ap, waits=()):
        vcopy(h_ap, src_ap, waits=waits)
        return tt(l_ap, src_ap, h_ap, ALU.subtract)

    def T_(name, t, w=None):
        w = TC if w is None else w
        return lambda name=name, t=t, w=w: sb[name][:, t * w:(t + 1) * w]

    def ccolap(off, i):
        return lambda off=off, i=i: sb["ccol"][:, off + i:off + i + 1]

    def miscap(i, p=128):
        return lambda i=i, p=p: sb["misc"][0:p, i:i + 1]

    def strip(name, s):
        return lambda name=name, s=s: sb[name][32 * (s % 4):32 * (s % 4) + 32,
                                               (s // 4) * TC:(s // 4) * TC + TC]

    cc_dtb, cc_D, cc_n1, cc_n2 = 0, DIN_T, 2 * DIN_T, 2 * DIN_T + KT

    wf32_tok = [None]
    pair_busy = {}

    def wload(wname, o, klo, khi, pair, waits=()):
        nk = khi - klo
        key = (wname, o, klo, khi)
        hD, lD = wcache[wname]
        hbuf, lbuf = pair
        waits = list(waits) + [pair_busy.pop(hbuf, None)]
        if key in cached:
            t1 = P.dma("qs", lambda b=hbuf, nk=nk: sb[b][:, 0:nk * 128],
                       lambda o=o, klo=klo, khi=khi, hD=hD:
                       hD.ap()[o, klo:khi].rearrange("k r c -> r k c"),
                       waits=list(waits) + [cached[key]])
            t2 = P.dma("qs", lambda b=lbuf, nk=nk: sb[b][:, 0:nk * 128],
                       lambda o=o, klo=klo, khi=khi, lD=lD:
                       lD.ap()[o, klo:khi].rearrange("k r c -> r k c"),
                       waits=[])
            return [t1, t2]
        t0 = P.dma("qs", lambda nk=nk: sb["wf32"][:, 0:nk * 128],
                   lambda o=o, klo=klo, khi=khi, w=wname:
                   io[w].ap()[o, klo:khi].rearrange("k r c -> r k c"),
                   waits=list(waits) + [wf32_tok[0]])
        vcopy(lambda b=hbuf, nk=nk: sb[b][:, 0:nk * 128],
              lambda nk=nk: sb["wf32"][:, 0:nk * 128], waits=[t0])
        c2 = tt(lambda b=lbuf, nk=nk: sb[b][:, 0:nk * 128],
                lambda nk=nk: sb["wf32"][:, 0:nk * 128],
                lambda b=hbuf, nk=nk: sb[b][:, 0:nk * 128], ALU.subtract)
        wf32_tok[0] = c2
        P.dma("qg", lambda o=o, klo=klo, khi=khi, hD=hD:
              hD.ap()[o, klo:khi].rearrange("k r c -> r k c"),
              lambda b=hbuf, nk=nk: sb[b][:, 0:nk * 128], waits=[c2])
        wtk = P.dma("qg", lambda o=o, klo=klo, khi=khi, lD=lD:
                    lD.ap()[o, klo:khi].rearrange("k r c -> r k c"),
                    lambda b=lbuf, nk=nk: sb[b][:, 0:nk * 128], waits=[])
        cached[key] = wtk
        pair_busy[hbuf] = wtk
        return [c2]

    def mm3(bank, pair, nk, srch, srcl, waits_first, first_start=True,
            last_stop=True, npass=3):
        hbuf, lbuf = pair
        first = [True]

        def w0():
            if first[0]:
                first[0] = False
                return list(waits_first)
            return []
        nmm = 0
        total = nk * npass
        for k in range(nk):
            wh = lambda b=hbuf, k=k: sb[b][:, k * 128:(k + 1) * 128]
            wl = lambda b=lbuf, k=k: sb[b][:, k * 128:(k + 1) * 128]
            seq = [(wh, srch[k]), (wh, srcl[k])] + ([(wl, srch[k])] if npass == 3 else [])
            for (wap, xap) in seq:
                nmm += 1
                mm(lambda bank=bank: ps[bank][:, 0:TC], wap, xap,
                   (nmm == 1 and first_start), (nmm == total and last_stop),
                   waits=w0())
        return P.tok("pe")


    P.dma("qs", lambda: sb["ones_k"][:], lambda: io["c_ones_k"].ap())
    P.dma("qs", lambda: sb["ones_1"][:], lambda: io["c_ones_1"].ap())
    P.dma("qs", lambda: sb["ep"][:], lambda: io["c_ep"].ap())
    P.dma("qs", lambda: sb["sr"][:], lambda: io["c_sr"].ap())
    P.dma("qs", lambda: sb["a_sc"][:], lambda: io["a_cols"].ap())
    P.dma("qs", lambda: sb["ccol"][:, cc_dtb:cc_dtb + DIN_T], lambda: io["c_dtb"].ap())
    P.dma("qs", lambda: sb["ccol"][:, cc_D:cc_D + DIN_T], lambda: io["c_D"].ap())
    P.dma("qs", lambda: sb["ccol"][:, cc_n1:cc_n1 + KT], lambda: io["c_n1w"].ap())
    P.dma("qs", lambda: sb["ccol"][:, cc_n2:cc_n2 + KT], lambda: io["c_n2w"].ap())
    P.dma("qs", lambda: sb["misc"][:], lambda: io["c_misc"].ap())
    tok_const = P.qtok("qs")
    act(lambda: sb["a_sc"][:], lambda: sb["a_sc"][:], AF.Exp, waits=[tok_const])
    P.op("vector", lambda e: e.tensor_scalar(sb["a_sc"][:], sb["a_sc"][:],
                                             -1.0, None, ALU.mult),
         incs=[("dv", 1)], waits=[P.tok("sc")])
    P.op("vector", lambda e: e.memset(sb["state"][:], 0.0), incs=[("dv", 1)])
    P.barrier()

    QW = 1024

    def qslice(which, w):
        return lambda which=which, w=w: sb["macc"][:, which * QW:which * QW + w]

    def quantize_weight(wname, qname, KS, OC):
        parts = [(p0, min(QW, OC - p0)) for p0 in range(0, OC, QW)]
        nslab = KS * len(parts)
        for k in range(KS):
            for hh, (p0, pw) in enumerate(parts):
                buf = (k * len(parts) + hh) % 2
                si = k * len(parts) + hh
                t_l = P.dma("qs", qslice(buf, pw),
                            lambda k=k, p0=p0, pw=pw: io[wname].ap()[k, :, p0:p0 + pw],
                            waits=([P.tok("sc")] if si >= 2 else
                                   [P.tok("dv"), P.qtok("qg")]))
                act(qslice(buf, pw), qslice(buf, pw),
                    AF.Abs, accum=lambda si=si: sb["gcol"][:, si:si + 1],
                    waits=[t_l] + ([P.tok("dv")] if si == 0 else []))
        P.op("vector", lambda e: e.reduce_sum(sb["stats"][:, 0:1],
                                              sb["gcol"][:, 0:nslab],
                                              axis=mybir.AxisListType.X),
             incs=[("dv", 1)], waits=[P.tok("sc"), P.tok("pe")])
        mm(lambda: ps["pb6"][0:1, 0:1], lambda: sb["ones_k"][:],
           lambda: sb["stats"][:, 0:1], True, True, waits=[P.tok("dv")])
        msl = 2 if OC == C.DFF else 4
        act(lambda: sb["stats"][0:1, 1:2], lambda: ps["pb6"][0:1, 0:1],
            AF.Identity, bias=miscap(3, 1), scale=miscap(msl, 1), waits=[P.tok("pe")])
        act(lambda: sb["stats"][0:1, 2:3], lambda: sb["stats"][0:1, 1:2],
            AF.Identity, scale=miscap(5, 1))
        mm(lambda: ps["pb7"][:, 0:1], lambda: sb["ones_1"][:],
           lambda: sb["stats"][0:1, 1:2], True, True, waits=[P.tok("sc")])
        a_t1 = act(lambda: sb["thr"][:, 0:1], lambda: ps["pb7"][:, 0:1],
                   AF.Identity, waits=[P.tok("pe"), P.tok("dv")])
        a_t2 = act(lambda: sb["thr"][:, 1:2], lambda: sb["thr"][:, 0:1],
                   AF.Identity, scale=miscap(5))
        tok_thr = a_t2
        for k in range(KS):
            for hh, (p0, pw) in enumerate(parts):
                buf = (k * len(parts) + hh) % 2
                t_l = P.dma("qs", qslice(buf, pw),
                            lambda k=k, p0=p0, pw=pw: io[wname].ap()[k, :, p0:p0 + pw],
                            waits=[P.qtok("qg"), P.tok("sc")])
                t2 = P.op("vector", lambda e, b=buf, pw=pw: e.tensor_scalar(
                    sb["macc"][:, 2 * QW:2 * QW + pw], qslice(b, pw)(), sb["thr"][:, 1:2],
                    None, ALU.is_le),
                    incs=[("dv", 1)], waits=[t_l, tok_thr, P.tok("dv")])
                qv = P.op("vector", lambda e, b=buf, pw=pw: e.scalar_tensor_tensor(
                    qslice(b, pw)(), qslice(b, pw)(), sb["thr"][:, 0:1],
                    sb["macc"][:, 2 * QW:2 * QW + pw], ALU.is_ge, ALU.subtract),
                    incs=[("dv", 1)])
                for oo in range(pw // 128):
                    o_glob = p0 // 128 + oo
                    P.dma("qg", lambda k=k, o=o_glob, q=qname: qscr[q].ap()[o, k],
                          lambda b=buf, oo=oo: sb["macc"][:, b * QW + oo * 128:b * QW + (oo + 1) * 128],
                          waits=[qv])

    quantize_weight("w_f1", "q1", KT, C.DFF)
    quantize_weight("w_f2", "q2", KT, C.DFF)
    quantize_weight("w_f3", "q3", C.KT_FF, C.DM)
    P.barrier()

    def emit_norm(src_tiles, dst_roll, kind, nw_off=None, post=None):
        for k in range(KT):
            a1 = act(T_("sq", k % 2), src_tiles[k], AF.Square,
                     waits=[P.tok("pe")] if k >= 2 else [])
            P.op("tensor", lambda e, k=k: e.matmul(
                ps["pb4"][0:1, 0:TC], sb["ones_k"][:],
                sb["sq"][:, (k % 2) * TC:(k % 2) * TC + TC],
                start=(k == 0), stop=(k == KT - 1)),
                incs=[("pe", 1)], waits=[a1])
            if kind == "ln":
                mm(lambda: ps["pb5"][0:1, 0:TC], lambda: sb["ones_k"][:],
                   src_tiles[k], k == 0, k == KT - 1)
        tok_mm = P.tok("pe")
        if kind == "ln":
            a_m = act(lambda: sb["srow"][0:1, 0:TC], lambda: ps["pb5"][0:1, 0:TC],
                      AF.Identity, scale=miscap(0, 1), waits=[tok_mm])
            act(lambda: sb["srow"][0:1, TC:2 * TC], lambda: sb["srow"][0:1, 0:TC],
                AF.Square)
            a_e = act(lambda: sb["srow"][0:1, 2 * TC:3 * TC],
                      lambda: ps["pb4"][0:1, 0:TC], AF.Identity, scale=miscap(0, 1))
            v_var = tt(lambda: sb["srow"][0:1, TC:2 * TC],
                       lambda: sb["srow"][0:1, 2 * TC:3 * TC],
                       lambda: sb["srow"][0:1, TC:2 * TC], ALU.subtract,
                       waits=[a_e])
            a_sd = act(lambda: sb["srow"][0:1, TC:2 * TC],
                       lambda: sb["srow"][0:1, TC:2 * TC], AF.Sqrt, bias=miscap(1, 1),
                       waits=[v_var])
        else:
            a_sd = act(lambda: sb["srow"][0:1, TC:2 * TC],
                       lambda: ps["pb4"][0:1, 0:TC], AF.Sqrt, bias=miscap(1, 1),
                       scale=miscap(0, 1), waits=[tok_mm])
        r_rc = P.op("vector", lambda e: e.reciprocal(
            sb["srow"][0:1, 3 * TC:4 * TC], sb["srow"][0:1, TC:2 * TC]),
            incs=[("dv", 1)], waits=[a_sd])
        mm(lambda: ps["pb4"][:, 0:TC], lambda: sb["ones_1"][:],
           lambda: sb["srow"][0:1, 3 * TC:4 * TC], True, True, waits=[r_rc])
        if kind == "ln":
            mm(lambda: ps["pb5"][:, 0:TC], lambda: sb["ones_1"][:],
               lambda: sb["srow"][0:1, 0:TC], True, True, waits=[a_m])
        tok_bc = P.tok("pe")
        for k in range(KT):
            if kind == "ln":
                tt(dst_roll[k], src_tiles[k], lambda: ps["pb5"][:, 0:TC],
                   ALU.subtract, waits=[tok_bc])
                tok = tt(dst_roll[k], dst_roll[k], lambda: ps["pb4"][:, 0:TC],
                         ALU.mult)
            else:
                tok = stt(dst_roll[k], src_tiles[k], ccolap(nw_off, k),
                          lambda: ps["pb4"][:, 0:TC], ALU.mult, ALU.mult,
                          waits=[tok_bc])
            if post is not None:
                post(k, tok)

    def norm_split(src_tiles, kind, nw_off=None):
        def post(k, tok):
            split(T_("sh", k), T_("sl", k), T_("u", k % 2))
        emit_norm(src_tiles, [T_("u", k % 2) for k in range(KT)], kind,
                  nw_off=nw_off, post=post)
        return P.tok("dv")

    def stream_proj(wname, o_list, srch, srcl, nk, evac, npass=3):
        pairs = [("whA", "wlA"), ("whB", "wlB")]
        wtok = [None, None]
        btok = [None, None]
        for idx, o in enumerate(o_list):
            par = idx % 2
            bank = f"pb{par}"
            tw = wload(wname, o, 0, nk, pairs[par],
                       waits=[wtok[par]] if wtok[par] else [])
            mm3(bank, pairs[par], nk, srch, srcl,
                waits_first=tw + [btok[par]], npass=npass)
            wtok[par] = P.tok("pe")
            btok[par] = evac(o, lambda bank=bank: ps[bank][:, 0:TC], P.tok("pe"))

    npre = C.NCH if n_pre_chunks is None else n_pre_chunks
    chunks = [("pre", i) for i in range(npre)] + [("own", i) for i in range(C.NCH)]

    for ckind, ci in chunks:
        own = ckind == "own"
        xsrc = "x_own" if own else "x_pre"
        c0 = ci * TC

        for k in range(KT):
            P.dma("qs", T_("xb", k),
                  lambda k=k, c0=c0, xs=xsrc: io[xs].ap()[k * 128:(k + 1) * 128,
                                                          c0:c0 + TC])
        P.barrier()
        emit_norm([T_("xb", k) for k in range(KT)],
                  [T_("ln1", k) for k in range(KT)], "ln")
        P.barrier()
        norm_split([T_("ln1", k) for k in range(KT)], "rms", nw_off=cc_n1)
        P.barrier()

        o_order = [C.o_dt] + list(range(C.o_x0, C.o_c0))
        srch_t = [T_("sh", k) for k in range(KT)]
        srcl_t = [T_("sl", k) for k in range(KT)]

        def evac_in(o, bank_ap, tok):
            if o == C.o_dt:
                vcopy(lambda: sb["dtraw"][:, 0:TC], bank_ap, waits=[tok])
                return split(lambda: sb["dtrh"][:], lambda: sb["dtrl"][:],
                             lambda: sb["dtraw"][:, 0:TC])
            if C.o_x0 <= o < C.o_b0:
                return vcopy(T_("xv", o - C.o_x0), bank_ap, waits=[tok])
            return vcopy(T_("bv", o - C.o_b0), bank_ap, waits=[tok])

        stream_proj("w_in", o_order, srch_t, srcl_t, KT, evac_in)
        P.barrier()

        for q in range(NQ):
            r2 = q % 2
            twd = wload("w_dt", q, 0, 1, ("whD", "wlD"), waits=[P.tok("pe")])
            mm3("pb1", ("whD", "wlD"), 1,
                [lambda: sb["dtrh"][:]], [lambda: sb["dtrl"][:]],
                waits_first=twd + [P.tok("sc")])
            tok_dtq = P.tok("pe")
            act(T_("dt", r2), lambda: ps["pb1"][:, 0:TC], AF.Exp,
                bias=ccolap(cc_dtb, q), waits=[tok_dtq])
            a_dt = act(T_("dt", r2), T_("dt", r2), AF.Ln, bias=1.0)
            s_dt = split(T_("dth", r2), T_("dtl", r2), T_("dt", r2),
                         waits=[a_dt])
            for r in range(4):
                P.dma("qg", lambda r=r, r2=r2: sb["xrep"][32 * r:32 * r + 32,
                                                          r2 * TC:r2 * TC + TC],
                      strip("xv", q), waits=[P.tok("dv")] if r == 0 else [])
            tok_xr = P.qtok("qg")
            tt(T_("dtx", r2), T_("dt", r2), T_("xrep", r2), ALU.mult,
               waits=[a_dt, tok_xr])
            s_dtx = split(T_("dtxh", r2), T_("dtxl", r2), T_("dtx", r2))
            for r in range(4):
                P.dma("qg", lambda q=q, r=r: sb["bbc"][32 * r:32 * r + 32,
                                                       (q % 2) * TC:(q % 2) * TC + TC],
                      strip("bv", q), waits=[P.tok("dv")] if r == 0 else [])
            if own:
                twz = wload("w_in", q, 0, KT, ("whA", "wlA"), waits=[P.tok("pe")])
                mm3("pb2", ("whA", "wlA"), KT, srch_t, srcl_t,
                    waits_first=twz + [P.tok("sc")])
                act(T_("zs", r2), lambda: ps["pb2"][:, 0:TC], AF.Silu,
                    waits=[P.tok("pe")])
                twc = wload("w_in", C.o_c0 + q, 0, KT, ("whB", "wlB"),
                            waits=[P.tok("pe")])
                mm3("pb3", ("whB", "wlB"), KT, srch_t, srcl_t,
                    waits_first=twc + [P.tok("dv")])
                v_c = vcopy(T_("cv", r2), lambda: ps["pb3"][:, 0:TC],
                            waits=[P.tok("pe")])
                for gg in range(4):
                    for r in range(4):
                        P.dma("qg", lambda gg=gg, r=r: sb["cbc"][32 * r:32 * r + 32,
                                                                 gg * TC:gg * TC + TC],
                              lambda gg=gg, r2=r2: sb["cv"][32 * gg:32 * gg + 32,
                                                            r2 * TC:r2 * TC + TC],
                              waits=[P.tok("gp"), v_c] if (gg == 0 and r == 0) else [])
            tok_cbc = P.qtok("qg")
            y_open = False
            for j in range(8):
                for gp in range(2):
                    toks_rep = []
                    for r in range(2):
                        g = 4 * q + 2 * gp + r
                        bA, bB = f"pb{4 + 2 * r}", f"pb{5 + 2 * r}"
                        sp = 32 * (g % 4)
                        epj = lambda j=j, sp=sp: sb["ep"][sp:sp + 32,
                                                          j * 128:(j + 1) * 128]
                        tp = (sp, 0)

                        def es(name, sp=sp, r2=r2):
                            return lambda name=name, sp=sp, r2=r2: \
                                sb[name][sp:sp + 32, r2 * TC:r2 * TC + TC]
                        P.op("tensor", lambda e, bA=bA, epj=epj, a1=es("dth"), tp=tp:
                             e.matmul(ps[bA][:, 0:TC], epj(), a1(), start=True,
                                      stop=False, tile_position=tp),
                             incs=[], waits=[P.tok("sc"), P.tok("dv"), s_dt])
                        P.op("tensor", lambda e, bA=bA, epj=epj, a2=es("dtl"), tp=tp:
                             e.matmul(ps[bA][:, 0:TC], epj(), a2(), start=False,
                                      stop=True, tile_position=tp),
                             incs=[("pe", 1)])
                        P.op("tensor", lambda e, bB=bB, epj=epj, a1=es("dtxh"), tp=tp:
                             e.matmul(ps[bB][:, 0:TC], epj(), a1(), start=True,
                                      stop=False, tile_position=tp),
                             incs=[], waits=[s_dtx])
                        P.op("tensor", lambda e, bB=bB, epj=epj, a2=es("dtxl"), tp=tp:
                             e.matmul(ps[bB][:, 0:TC], epj(), a2(), start=False,
                                      stop=True, tile_position=tp),
                             incs=[("pe", 1)])
                        toks_rep.append((g, bA, bB, P.tok("pe")))
                    for ri, (g, bA, bB, tok_r) in enumerate(toks_rep):
                        a1 = act(T_("ab", ri), lambda bA=bA: ps[bA][:, 0:TC], AF.Exp,
                                 scale=lambda g=g, j=j: sb["a_sc"][:, 8 * g + j:8 * g + j + 1],
                                 waits=[tok_r])
                        z1 = tt(T_("zb", ri), lambda bB=bB: ps[bB][:, 0:TC],
                                T_("bbc", q % 2), ALU.mult, waits=[tok_r, tok_cbc])
                        h1 = P.op("vector", lambda e, ri=ri, g=g, j=j: e.tensor_tensor_scan(
                            sb["hb"][:, ri * TC:ri * TC + TC],
                            sb["ab"][:, ri * TC:ri * TC + TC],
                            sb["zb"][:, ri * TC:ri * TC + TC],
                            sb["state"][:, 8 * g + j:8 * g + j + 1],
                            ALU.mult, ALU.add),
                            incs=[("dv", 1)], waits=[a1, z1])
                        vcopy(lambda g=g, j=j: sb["state"][:, 8 * g + j:8 * g + j + 1],
                              lambda ri=ri: sb["hb"][:, ri * TC + TC - 1:ri * TC + TC])
                        if own:
                            mmul = tt(T_("mb", ri), T_("hb", ri),
                                      T_("cbc", g - 4 * q), ALU.mult,
                                      waits=[h1, tok_cbc], eng="gpsimd", sem="gp")
                            rr = g - 4 * q
                            off = j * 224 + 96 - 32 * rr
                            mm(lambda: ps["pb0"][:, 0:TC],
                               lambda off=off: sb["sr"][:, off:off + 128],
                               T_("mb", ri), not y_open,
                               (j == 7 and gp == 1 and ri == 1), waits=[mmul])
                            y_open = True
            if own:
                tok_y = P.tok("pe")
                stt(T_("y2", r2), T_("xrep", r2), ccolap(cc_D, q),
                    lambda: ps["pb0"][:, 0:TC], ALU.mult, ALU.add,
                    waits=[tok_y, P.tok("sc")])
                tt(T_("y2", r2), T_("y2", r2), T_("zs", r2), ALU.mult)
                split(T_("yh", q), T_("yl", q), T_("y2", r2))
        P.barrier()

        if not own:
            continue

        def evac_o(o, bank_ap, tok):
            return tt(T_("ln1", o), T_("ln1", o), bank_ap, ALU.add, waits=[tok])

        stream_proj("w_o", list(range(KT)),
                    [T_("yh", k) for k in range(DIN_T)],
                    [T_("yl", k) for k in range(DIN_T)], DIN_T, evac_o)
        P.barrier()

        norm_split([T_("ln1", k) for k in range(KT)], "rms", nw_off=cc_n2)
        for k in range(KT):
            tt(T_("macc", k), T_("xb", k), T_("ln1", k), ALU.add)
        P.barrier()

        def gated_ffn(wg, wu, wd, actf, acc_tiles, qn=None):
            HB = 8
            nhb = (C.OT_FF + HB - 1) // HB
            gpair = [("whA", "wlA"), ("whB", "wlB")]
            upair = [("whU", "wlU"), ("whV", "wlV")]
            dpair = [("whD", "wlD"), ("whE", "wlE")]
            np_ = 2 if qn else 3
            wtok = [None, None]
            utok = [None, None]
            dtok = [None, None]
            hbank = [None, None]
            for hb in range(nhb):
                o_lo, o_hi = hb * HB, min(C.OT_FF, hb * HB + HB)
                for o in range(o_lo, o_hi):
                    par = o % 2
                    bA, bB = f"pb{2 + par}", f"pb{4 + par}"
                    if qn is None:
                        twg = wload(wg, o, 0, KT, gpair[par],
                                    waits=[wtok[par]] if wtok[par] else [])
                        twu = wload(wu, o, 0, KT, upair[par],
                                    waits=[utok[par]] if utok[par] else [])
                    else:
                        twg = [P.dma("qs", lambda b=gpair[par][0]: sb[b][:, 0:KT * 128],
                                     lambda o=o: qscr[qn[0]].ap()[o].rearrange("k r c -> r k c"),
                                     waits=[wtok[par]] if wtok[par] else [])]
                        twu = [P.dma("qs", lambda b=upair[par][0]: sb[b][:, 0:KT * 128],
                                     lambda o=o: qscr[qn[1]].ap()[o].rearrange("k r c -> r k c"),
                                     waits=[utok[par]] if utok[par] else [])]
                    mm3(bA, gpair[par], KT, srch_t, srcl_t,
                        waits_first=twg + [hbank[par]], npass=np_)
                    wtok[par] = P.tok("pe")
                    mm3(bB, upair[par], KT, srch_t, srcl_t,
                        waits_first=twu + [hbank[par]], npass=np_)
                    utok[par] = P.tok("pe")
                    a1 = act(T_("sg", par), lambda bA=bA: ps[bA][:, 0:TC], actf,
                             waits=[utok[par]])
                    hv = tt(T_("hid", par), T_("sg", par),
                            lambda bB=bB: ps[bB][:, 0:TC], ALU.mult, waits=[a1])
                    hbank[par] = split(T_("hh2", o - o_lo), T_("hl2", o - o_lo),
                                       T_("hid", par))
                tok_hid = P.tok("dv")
                nk3 = o_hi - o_lo
                for o2 in range(KT):
                    par = o2 % 2
                    bank = f"pb{par}"
                    if qn is None:
                        tw3 = wload(wd, o2, o_lo, o_hi, dpair[par],
                                    waits=[dtok[par]] if dtok[par] else [])
                    else:
                        tw3 = [P.dma("qs", lambda b=dpair[par][0], nk3=nk3: sb[b][:, 0:nk3 * 128],
                                     lambda o2=o2, lo=o_lo, hi=o_hi: qscr[qn[2]].ap()[o2, lo:hi].rearrange("k r c -> r k c"),
                                     waits=[dtok[par]] if dtok[par] else [])]
                    mm3(bank, dpair[par], nk3,
                        [T_("hh2", kk) for kk in range(nk3)],
                        [T_("hl2", kk) for kk in range(nk3)],
                        waits_first=tw3 + [tok_hid, dtok[par]], npass=np_)
                    dtok[par] = tt(acc_tiles[o2], acc_tiles[o2],
                                   lambda bank=bank: ps[bank][:, 0:TC], ALU.add,
                                   waits=[P.tok("pe")])
                P.barrier()

        gated_ffn("w_g", "w_u", "w_d", AF.Silu, [T_("macc", k) for k in range(KT)])
        P.barrier()

        def post_ln2(k, tok):
            split(T_("sh", k), T_("sl", k), T_("ln1", k))
        emit_norm([T_("macc", k) for k in range(KT)],
                  [T_("ln1", k) for k in range(KT)], "ln", post=post_ln2)
        P.barrier()
        gated_ffn(None, None, None, AF.Sigmoid, [T_("macc", k) for k in range(KT)],
                  qn=("q1", "q2", "q3"))
        P.barrier()

        for k in range(KT):
            P.dma("qg", lambda k=k, c0=c0: io["y_out"].ap()[k * 128:(k + 1) * 128,
                                                            c0:c0 + TC],
                  T_("macc", k))
        P.barrier()

    for name, (dr, key) in dbg.items():
        P.dma("qg", lambda dr=dr: dr.ap(), lambda key=key: sb[key][:])
    P.wait("gpsimd", P.qtok("qg"), P.qtok("qs"), P.qtok("qa"))

    with ctx:
        with ExitStack() as sctx:
            sems = {s: sctx.enter_context(nc.semaphore(s)) for s in ESEMS}
            with nc.Block() as block:
                @block.sync
                def _(e):
                    for fn in P.sched["sync"]:
                        fn(e, sems)

                @block.tensor
                def _(e):
                    for fn in P.sched["tensor"]:
                        fn(e, sems)

                @block.vector
                def _(e):
                    for fn in P.sched["vector"]:
                        fn(e, sems)

                @block.scalar
                def _(e):
                    for fn in P.sched["scalar"]:
                        fn(e, sems)

                @block.gpsimd
                def _(e):
                    for fn in P.sched["gpsimd"]:
                        fn(e, sems)
    return nc



def kernel(**inputs):
    C = CFG()
    nc = build(C)
    wts = host_weights(C, inputs)
    consts = host_constants(C)
    in_maps = []
    for core in range(8):
        m = dict(wts)
        m.update(consts)
        m.update(host_core_inputs(C, inputs, core))
        in_maps.append(m)
    res = run_bass_kernel_spmd(nc, in_maps, core_ids=list(range(8)))
    out = np.empty((4, 2048, 2048), np.float32)
    for core in range(8):
        b, half = divmod(core, 2)
        y = np.asarray(res.results[core]["y_out"])
        out[b, C.TOWN * half:C.TOWN * (half + 1), :] = y.T
    return out
